# version 16
# baseline (speedup 1.0000x reference)
import numpy as np
from contextlib import ExitStack

import concourse.bass as bass
import concourse.mybir as mybir
from concourse.bass_utils import run_bass_kernel_spmd
from concourse.bass import IndirectOffsetOnAxis

F32 = mybir.dt.float32
BF16 = mybir.dt.bfloat16
FP8 = mybir.dt.float8e4
I32 = mybir.dt.int32
U32 = mybir.dt.uint32
AF = mybir.ActivationFunctionType
ALU = mybir.AluOpType
AX = mybir.AxisListType

S = 2048
D = 1024
H = 16
DH = 64
E = 16
TT = 16
DC = 8
QCW = 512
NQC = S // QCW
NQT = QCW // 128
KT = 16
NMI = 100000
MREG = NQC
MCAP = 26624
NMI2 = MREG * MCAP
MIB = NMI2 // 128
RB = MIB // MREG
SCALE = 0.125
ROUTE_SCALE = 2.826
EPS = 1e-5
TOPK = 8
TCW = 512
NTC = S // TCW

ENGS = ("sync", "tensor", "vector", "scalar", "gpsimd")
PROD = {"pe": "tensor", "dve": "vector", "act": "scalar", "pool": "gpsimd"}


class Prog:
    SELFCHAIN = {"vector": "dve"}

    def __init__(self):
        self.q = {e: [] for e in ENGS}
        self.cnt = {}
        self.lastwait = {e: {} for e in ENGS}

    def op(self, eng, fn, inc=None):
        sem = self.SELFCHAIN.get(eng)
        if sem is not None:
            cur = self.cnt.get(sem, 0)
            if cur > 0 and self.lastwait[eng].get(sem, 0) < cur:
                self.q[eng].append((("wait", sem, cur), None))
                self.lastwait[eng][sem] = cur
            if inc is None:
                inc = (sem, 1)
            else:
                assert inc[0] == sem, (eng, inc)
        self.q[eng].append((fn, inc))
        if inc:
            self.cnt[inc[0]] = self.cnt.get(inc[0], 0) + inc[1]
        return self.cnt.get(inc[0], 0) if inc else None

    def now(self, sem):
        return self.cnt.get(sem, 0)

    def wait(self, eng, sem, val=None):
        if val is None:
            val = self.cnt.get(sem, 0)
        if val <= 0:
            return
        if PROD.get(sem) == eng:
            return
        if self.lastwait[eng].get(sem, 0) >= val:
            return
        self.lastwait[eng][sem] = val
        self.q[eng].append((("wait", sem, val), None))

    def barrier(self):
        for e in ENGS:
            for s in list(self.cnt):
                self.wait(e, s)


def build(taps=(), stop=None, n_experts=E + 1):
    nc = bass.Bass("TRN2", target_bir_lowering=False, detect_race_conditions=False)

    x_d = nc.dram_tensor("x", [S, D], F32, kind="ExternalInput").ap()
    minds_d = nc.dram_tensor("minds", [2, NMI2], I32, kind="ExternalInput").ap()
    wqT_d = nc.dram_tensor("wqT", [D, D], F32, kind="ExternalInput").ap()
    wkT_d = nc.dram_tensor("wkT", [D, D], F32, kind="ExternalInput").ap()
    wvT_d = nc.dram_tensor("wvT", [D, D], F32, kind="ExternalInput").ap()
    woT_d = nc.dram_tensor("woT", [D, D], F32, kind="ExternalInput").ap()
    gwT_d = nc.dram_tensor("gwT", [D, E], F32, kind="ExternalInput").ap()
    ebias_d = nc.dram_tensor("ebias", [1, E], F32, kind="ExternalInput").ap()
    wgT_d = nc.dram_tensor("wgT", [E, D, D], F32, kind="ExternalInput").ap()
    wuT_d = nc.dram_tensor("wuT", [E, D, D], F32, kind="ExternalInput").ap()
    wdT_d = nc.dram_tensor("wdT", [E, D, D], F32, kind="ExternalInput").ap()
    sgT_d = nc.dram_tensor("sgT", [D, D], F32, kind="ExternalInput").ap()
    suT_d = nc.dram_tensor("suT", [D, D], F32, kind="ExternalInput").ap()
    sdT_d = nc.dram_tensor("sdT", [D, D], F32, kind="ExternalInput").ap()
    out_d = nc.dram_tensor("out", [S, D], F32, kind="ExternalOutput").ap()
    mask_d = nc.dram_tensor("mask_hbm", [S + 1, S], FP8, kind="Internal").ap()
    wsp_d = nc.dram_tensor("wsp_hbm", [(E + 1) * 3 * DC, 128, 2048], BF16, kind="Internal").ap()
    tap_d = {}
    for name, shape in dict(
        ln1T=[128, DC * S], z_all=[128, TT * D], res_attn=[128, TT * D],
        ln2T=[128, DC * S], combine=[128, TT * E], q0=[64, S], v0=[128, TT * 65],
        htld0=[128, DC * TCW // 2], res_ffn=[128, TT * D],
    ).items():
        if name in taps:
            tap_d[name] = nc.dram_tensor("tap_" + name, shape, F32, kind="ExternalOutput").ap()

    p = Prog()

    with ExitStack() as ctx:
        def sbuf(name, shape, dtype=F32):
            return ctx.enter_context(nc.sbuf_tensor(name, shape, dtype))

        fm = sbuf("fm", [128, DC * S])
        tm = sbuf("tm", [128, TT * D])
        fmH = tm[:, 0:TT * D // 2].bitcast(BF16)
        fmL = tm[:, TT * D // 2:TT * D].bitcast(BF16)
        X = sbuf("X", [128, 12800])
        xt = [X[:, i * 1024:(i + 1) * 1024] for i in range(2)]
        woS = X[:, 2048:2048 + 4096]
        qT = X[:, 0:2048]
        kT_ = X[:, 2048:4096]
        qTH = X[:, 0:1024].bitcast(BF16)
        qTL = X[:, 1024:2048].bitcast(BF16)
        kTH = X[:, 2048:3072].bitcast(BF16)
        kTL = X[:, 3072:4096].bitcast(BF16)
        vA = X[:, 4096:6144]
        wqS = [X[:, 6144 + i * 512:6144 + (i + 1) * 512] for i in range(2)]
        wkS = [X[:, 7168 + i * 512:7168 + (i + 1) * 512] for i in range(2)]
        wvS = [X[:, 8192 + i * 512:8192 + (i + 1) * 512] for i in range(2)]
        zstage = X[:, 11776:12288]
        srow = X[0:1, 12288:12800]
        bcS = X[:, 12288:12800]
        wgHL = [X[:, i * 1024:(i + 1) * 1024].bitcast(BF16) for i in range(2)]
        wuHL = [X[:, 2048 + i * 1024:2048 + (i + 1) * 1024].bitcast(BF16) for i in range(2)]
        wdHL = [X[:, 4096 + i * 1024:4096 + (i + 1) * 1024].bitcast(BF16) for i in range(2)]
        wgH = [w[:, 0:1024] for w in wgHL]
        wgL = [w[:, 1024:2048] for w in wgHL]
        wuH = [w[:, 0:1024] for w in wuHL]
        wuL = [w[:, 1024:2048] for w in wuHL]
        wdH = [w[:, 0:1024] for w in wdHL]
        wdL = [w[:, 1024:2048] for w in wdHL]
        htldF = X[:, 6144:6656]
        htldH = X[:, 6656:8704].bitcast(BF16)
        htldL = X[:, 8704:10752].bitcast(BF16)
        ps_stage = X[:, 9216:10240]
        ps_hilo = X[:, 10240:11264].bitcast(BF16)

        pbuf = sbuf("pbuf", [128, 4 * QCW])
        keep8 = sbuf("keep8", [128, 2 * KT * QCW], FP8)
        offs = sbuf("offs", [128, MIB], U32)
        offf = X[:, 2048:2880]
        idxf = X[:, 2880:3712]
        ones8 = X[:, 6144:6656].bitcast(FP8)
        zero8 = sbuf("zero8", [128, 1], FP8)
        ident = sbuf("ident", [128, 128])
        onesr = sbuf("onesr", [1, 128])
        bnbuf = sbuf("bnbuf", [128, 12])
        mv = sbuf("mv", [128, 2])
        m1 = sbuf("m1", [128, TT])
        r1 = sbuf("r1", [128, TT])
        m2 = sbuf("m2", [128, TT])
        r2 = sbuf("r2", [128, TT])
        lntile = X[:, 11776:12800]
        rtile = sbuf("rtile", [128, 2])
        ebias_s = sbuf("ebias_s", [1, E])
        c_eps = sbuf("c_eps", [128, 1])
        biasbc = sbuf("biasbc", [128, E])
        sctile = sbuf("sctile", [128, E])
        seltile = sbuf("seltile", [128, E])
        work = sbuf("work", [128, E])
        eqt = sbuf("eqt", [128, E])
        t8 = sbuf("t8", [128, 1])
        s8 = sbuf("s8", [128, 1])
        comb = X[:, 11264:11520]
        gwS = X[:, 11520:11648]
        gwH = X[:, 11648:11712].bitcast(BF16)
        gwL = X[:, 11712:11776].bitcast(BF16)

        ps = [ctx.enter_context(nc.psum_tensor(f"ps{i}", [128, 512], F32)) for i in range(8)]

        sems = {}
        for sname in ("qs", "qa", "qg", "pe", "dve", "act", "pool", "sc"):
            sems[sname] = ctx.enter_context(nc.semaphore("sem_" + sname))

        block = ctx.enter_context(nc.Block())

        def OP(eng, fn, inc=None):
            return p.op(eng, fn, inc)

        def W(eng, sem, val=None):
            p.wait(eng, sem, val)

        def tap_flush():
            for _e in ENGS:
                p.wait(_e, "qs")

        STOP_ORDER = ["A", "B", "B2", "CD", "E", None]

        def go(tag):
            if stop is None:
                return True
            return STOP_ORDER.index(tag) <= STOP_ORDER.index(stop)

        OP("gpsimd", lambda e: e.memset(ones8[:, :], 1.0))
        OP("gpsimd", lambda e: e.memset(zero8[:, :], 0.0))
        OP("gpsimd", lambda e: e.memset(onesr[:, :], 1.0))
        OP("gpsimd", lambda e: e.memset(c_eps[:, :], EPS))
        OP("gpsimd", lambda e: e.memset(vA[:, :], 0.0))
        for t in range(TT):
            OP("gpsimd", lambda e, t=t: e.memset(vA[:, t * 128:t * 128 + 1], 1.0))
        OP("gpsimd", lambda e: e.iota(ident[:, :], [[-1, 128]], base=0, channel_multiplier=1,
                                      allow_small_or_imprecise_dtypes=True), inc=("pool", 1))
        pool_iota = p.now("pool")
        W("vector", "pool", pool_iota)
        OP("vector", lambda e: e.tensor_scalar(ident[:, :], ident[:, :], 0.0, None, ALU.is_equal))
        dve_ident = p.now("dve")

        OP("sync", lambda e: e.dma_start(
            offf[:, :].bitcast(I32),
            minds_d[0:1, :].rearrange("o (pp c) -> (o pp) c", pp=128)), inc=("qs", 16))
        OP("sync", lambda e: e.dma_start(
            idxf[:, :].bitcast(I32),
            minds_d[1:2, :].rearrange("o (pp c) -> (o pp) c", pp=128)), inc=("qs", 16))
        qs_minds = p.now("qs")

        W("scalar", "pool", pool_iota)
        for i in range(TT):
            OP("scalar", lambda e, i=i: e.dma_start(mask_d[i * 128:(i + 1) * 128, :], ones8[:, :]),
               inc=("qa", 16))
        OP("scalar", lambda e: e.dma_start(mask_d[S:S + 1, :], ones8[0:1, :]), inc=("qa", 16))
        qa_maskinit = p.now("qa")

        W("vector", "qs", qs_minds)
        OP("vector", lambda e: e.tensor_copy(offf[:, :], offf[:, :].bitcast(I32)))
        OP("vector", lambda e: e.tensor_copy(idxf[:, :], idxf[:, :].bitcast(I32)))
        OP("vector", lambda e: e.scalar_tensor_tensor(offf[:, :], idxf[:, :], float(S),
                                                      offf[:, :], ALU.mult, ALU.add))
        OP("vector", lambda e: e.tensor_copy(offs[:, :], offf[:, :]))
        dve_offs = p.now("dve")

        W("gpsimd", "dve", dve_offs)
        W("gpsimd", "qa", qa_maskinit)
        sc_thr = []
        for c in range(MIB):
            OP("gpsimd", lambda e, c=c: e.indirect_dma_start(
                mask_d.flatten().unsqueeze(1),
                IndirectOffsetOnAxis(ap=offs[:, c:c + 1], axis=0),
                zero8[:, 0:1], None), inc=("sc", 16))
            if (c + 1) % RB == 0:
                sc_thr.append(p.now("sc"))
        assert len(sc_thr) == MREG

        xt_mark = [0, 0]
        trb = [5, 6]
        trbA = [0, 0]
        trbV = [0, 0]
        lnt_mark = 0
        for t in range(TT):
            pr = t % 2
            W("scalar", "dve", xt_mark[pr])
            OP("scalar", lambda e, t=t, pr=pr: e.dma_start(xt[pr], x_d[t * 128:(t + 1) * 128, :]),
               inc=("qa", 16))
            qa_x = p.now("qa")
            W("vector", "qa", qa_x)
            OP("vector", lambda e, pr=pr: e.bn_stats(bnbuf[:, 0:6], xt[pr][:, 0:512]))
            OP("vector", lambda e, pr=pr: e.bn_stats(bnbuf[:, 6:12], xt[pr][:, 512:1024]))
            OP("vector", lambda e: e.bn_aggr(mv[:, :], bnbuf[:, :]))
            W("scalar", "dve", p.now("dve"))
            OP("scalar", lambda e: e.activation(rtile[:, 0:1], mv[:, 1:2], AF.Sqrt, bias=c_eps[:, 0:1]),
               inc=("act", 1))
            W("vector", "act", p.now("act"))
            OP("vector", lambda e, t=t: e.reciprocal(r1[:, t:t + 1], rtile[:, 0:1]))
            OP("vector", lambda e, t=t: e.tensor_copy(m1[:, t:t + 1], mv[:, 0:1]))
            W("vector", "pe", lnt_mark)
            OP("vector", lambda e, t=t, pr=pr: e.tensor_scalar(
                lntile[:, :], xt[pr][:, :], m1[:, t:t + 1], r1[:, t:t + 1],
                ALU.subtract, ALU.mult))
            xt_mark[pr] = p.now("dve")
            dve_ln = p.now("dve")
            for dc in range(DC):
                b = trb[dc % 2]
                W("tensor", "act", trbA[dc % 2])
                W("tensor", "dve", dve_ln)
                W("tensor", "dve", dve_ident)
                W("tensor", "pool", pool_iota)
                OP("tensor", lambda e, b=b, dc=dc: e.transpose(
                    ps[b][:, 0:128], lntile[:, dc * 128:(dc + 1) * 128], ident[:, :]),
                   inc=("pe", 1))
                W("scalar", "pe", p.now("pe"))
                OP("scalar", lambda e, b=b, dc=dc, t=t: e.copy(
                    fm[:, dc * S + t * 128: dc * S + (t + 1) * 128], ps[b][:, 0:128]),
                   inc=("act", 1))
                trbA[dc % 2] = p.now("act")
            lnt_mark = p.now("pe")
        act_ln1T_done = p.now("act")
        if "ln1T" in tap_d:
            p.barrier()
            OP("sync", lambda e: e.dma_start(tap_d["ln1T"], fm[:, :]), inc=("qs", 16))
            tap_flush()

        psp_steps = []
        wsp_ready = {}
        psp_state = {"stage_free": None, "hilo_free": 0}

        def _psp_emit(widx, kind, e_i, ic):
            def load():
                if kind == "g":
                    srcs = sgT_d if e_i == E else wgT_d[e_i]
                elif kind == "u":
                    srcs = suT_d if e_i == E else wuT_d[e_i]
                else:
                    srcs = sdT_d if e_i == E else wdT_d[e_i]
                if kind == "d":
                    OP("gpsimd", lambda e, srcs=srcs, ic=ic: e.dma_start(
                        ps_stage[:, :], srcs[ic * 128:(ic + 1) * 128, :]), inc=("qg", 16))
                else:
                    OP("gpsimd", lambda e, srcs=srcs, ic=ic: e.dma_start(
                        ps_stage[:, :].rearrange("p (dc w) -> p dc w", w=128),
                        srcs.rearrange("(dc p) (ic w) -> p dc ic w", p=128, w=128)[:, :, ic, :]),
                       inc=("qg", 16))
                W("gpsimd", "qg", p.now("qg"))
                OP("gpsimd", lambda e: e.tensor_copy(ps_hilo[:, 0:1024], ps_stage[:, :]))
                OP("gpsimd", lambda e: e.tensor_tensor(
                    ps_hilo[:, 1024:2048], ps_stage[:, :], ps_hilo[:, 0:1024],
                    ALU.subtract), inc=("pool", 1))
                OP("gpsimd", lambda e, widx=widx: e.dma_start(wsp_d[widx], ps_hilo[:, :]),
                   inc=("qg", 16))
                psp_state["hilo_free"] = p.now("qg")
                wsp_ready[widx] = p.now("qg")
            return load

        _order = []
        for _e in range(E + 1):
            for _ic in range(DC):
                _order.append(("g", _e, _ic))
                _order.append(("u", _e, _ic))
            for _ic in range(DC):
                _order.append(("d", _e, _ic))
        for _kind, _e, _ic in _order:
            _w = {"g": 0, "u": 1, "d": 2}[_kind]
            _widx = (_e * 3 + _w) * DC + _ic
            psp_steps.append(_psp_emit(_widx, _kind, _e, _ic))
        psp_pos = [0]

        def psp_step(n):
            while psp_pos[0] < len(psp_steps) and n > 0:
                psp_steps[psp_pos[0]]()
                psp_pos[0] += 1
                n -= 1

        def psp_flush():
            psp_step(len(psp_steps))

        if go("B"):
            wq_mark = [0, 0]
            keep_mark = [0, 0]
            pb_mark = [0, 0, 0, 0]
            scb_mark = [0, 0, 0, 0]
            zb_mark = [0, 0]
            projb_mark = [0, 0]
            projbV_mark = [0, 0]
            zst_free = [0]
            bcS_free = [0]

            for h in range(H):
                hp = h % 2
                W("sync", "pe", wq_mark[hp])
                if h == 0:
                    W("sync", "qa", qa_maskinit)
                OP("sync", lambda e, h=h, hp=hp: e.dma_start(
                    wqS[hp][:, :].rearrange("p (dc w) -> p dc w", w=64),
                    wqT_d.rearrange("(dc p) (hh w) -> p dc hh w", p=128, w=64)[:, :, h, :]),
                   inc=("qs", 16))
                OP("sync", lambda e, h=h, hp=hp: e.dma_start(
                    wkS[hp][:, :].rearrange("p (dc w) -> p dc w", w=64),
                    wkT_d.rearrange("(dc p) (hh w) -> p dc hh w", p=128, w=64)[:, :, h, :]),
                   inc=("qs", 16))
                OP("sync", lambda e, h=h, hp=hp: e.dma_start(
                    wvS[hp][:, :].rearrange("p (dc w) -> p dc w", w=64),
                    wvT_d.rearrange("(dc p) (hh w) -> p dc hh w", p=128, w=64)[:, :, h, :]),
                   inc=("qs", 16))
                qs_w = p.now("qs")

                for which, wS, dstH, dstL in (("q", wqS, qTH, qTL), ("k", wkS, kTH, kTL)):
                    for qc4 in range(4):
                        bpar = qc4 % 2
                        b = 5 + bpar
                        W("tensor", "qs", qs_w)
                        W("tensor", "act", act_ln1T_done)
                        W("tensor", "act", projb_mark[bpar])
                        W("tensor", "dve", projbV_mark[bpar])
                        for dc in range(DC):
                            OP("tensor", lambda e, b=b, dc=dc, qc4=qc4, wS=wS, hp=hp: e.matmul(
                                ps[b][0:64, :], wS[hp][:, dc * 64:(dc + 1) * 64],
                                fm[:, dc * S + qc4 * 512:dc * S + (qc4 + 1) * 512],
                                start=(dc == 0), stop=(dc == DC - 1)),
                               inc=("pe", 1) if dc == DC - 1 else None)
                        pe_proj = p.now("pe")
                        W("scalar", "pe", pe_proj)
                        if h > 0:
                            W("scalar", "pe", qk_evac_guard)
                        OP("scalar", lambda e, b=b, qc4=qc4, dstH=dstH: e.copy(
                            dstH[0:64, qc4 * 512:(qc4 + 1) * 512], ps[b][0:64, :]), inc=("act", 1))
                        projb_mark[bpar] = p.now("act")
                        W("vector", "pe", pe_proj)
                        W("vector", "act", projb_mark[bpar])
                        OP("vector", lambda e, b=b, qc4=qc4, dstH=dstH, dstL=dstL:
                           e.scalar_tensor_tensor(
                               dstL[0:64, qc4 * 512:(qc4 + 1) * 512],
                               dstH[0:64, qc4 * 512:(qc4 + 1) * 512], -1.0,
                               ps[b][0:64, :], ALU.mult, ALU.add))
                        projbV_mark[bpar] = p.now("dve")
                for t in range(TT):
                    bpar = t % 2
                    b = 5 + bpar
                    W("tensor", "act", projb_mark[bpar])
                    W("tensor", "dve", projbV_mark[bpar])
                    for dc in range(DC):
                        OP("tensor", lambda e, b=b, dc=dc, t=t, hp=hp: e.matmul(
                            ps[b][:, 0:64], fm[:, dc * S + t * 128:dc * S + (t + 1) * 128],
                            wvS[hp][:, dc * 64:(dc + 1) * 64],
                            start=(dc == 0), stop=(dc == DC - 1)),
                           inc=("pe", 1) if dc == DC - 1 else None)
                    W("scalar", "pe", p.now("pe"))
                    OP("scalar", lambda e, b=b, t=t: e.copy(
                        vA[:, t * 128 + 64:t * 128 + 128], ps[b][:, 0:64]), inc=("act", 1))
                    projb_mark[bpar] = p.now("act")
                act_proj_done = p.now("act")
                dve_proj_done = p.now("dve")

                if h == 0 and "v0" in tap_d:
                    p.barrier()
                    OP("sync", lambda e: e.dma_start(tap_d["v0"], vA[:, :]), inc=("qs", 16))
                    tap_flush()

                for qc in range(NQC):
                    kp = qc % 2
                    W("sync", "dve", keep_mark[kp])
                    W("sync", "sc", sc_thr[qc])
                    OP("sync", lambda e, qc=qc, kp=kp: e.dma_start(
                        keep8[:, kp * KT * QCW:(kp + 1) * KT * QCW].rearrange(
                            "p (kt w) -> p kt w", w=QCW),
                        mask_d[0:S, :].rearrange("(kt p) (nq w) -> p kt nq w", p=128, w=QCW)[:, :, qc, :]),
                       inc=("qs", 16))
                    qs_keep = p.now("qs")
                    psp_step(7)
                    dve_pm_k = {}
                    zb = 3 + (qc % 2)
                    ob = 3 + ((qc + 1) % 2)

                    def emit_av(kt2):
                        i3b = kt2 % 4
                        W("tensor", "dve", dve_pm_k[kt2])
                        if kt2 == 0:
                            W("tensor", "dve", zb_mark[qc % 2])
                        OP("tensor", lambda e, i3b=i3b, kt2=kt2, zb=zb: e.matmul(
                            ps[zb][:, 0:QCW], vA[:, kt2 * 128:(kt2 + 1) * 128],
                            pbuf[:, i3b * QCW:(i3b + 1) * QCW],
                            start=(kt2 == 0), stop=(kt2 == KT - 1)),
                           inc=("pe", 1))
                        pb_mark[i3b] = p.now("pe")

                    for kt in range(KT):
                        i3 = kt % 4
                        sb_b = [0, 1, 2, 7][i3]
                        W("tensor", "act", scb_mark[i3])
                        W("tensor", "act", act_proj_done)
                        W("tensor", "dve", dve_proj_done)
                        for pi, (lK, lQ) in enumerate(((kTH, qTH), (kTH, qTL), (kTL, qTH))):
                            OP("tensor", lambda e, sb_b=sb_b, kt=kt, qc=qc, lK=lK, lQ=lQ, pi=pi:
                               e.matmul(
                                   ps[sb_b][:, 0:QCW], lK[0:64, kt * 128:(kt + 1) * 128],
                                   lQ[0:64, qc * QCW:(qc + 1) * QCW],
                                   start=(pi == 0), stop=(pi == 2)),
                               inc=("pe", 1) if pi == 2 else None)
                        pe_sc = p.now("pe")
                        W("scalar", "pe", pe_sc)
                        W("scalar", "pe", pb_mark[i3])
                        OP("scalar", lambda e, i3=i3, sb_b=sb_b: e.activation(
                            pbuf[:, i3 * QCW:(i3 + 1) * QCW], ps[sb_b][:, 0:QCW], AF.Exp, scale=SCALE),
                           inc=("act", 1))
                        scb_mark[i3] = p.now("act")
                        W("vector", "act", p.now("act"))
                        W("vector", "qs", qs_keep)
                        OP("vector", lambda e, i3=i3, kt=kt, kp=kp: e.tensor_tensor(
                            pbuf[:, i3 * QCW:(i3 + 1) * QCW], pbuf[:, i3 * QCW:(i3 + 1) * QCW],
                            keep8[:, (kp * KT + kt) * QCW:(kp * KT + kt + 1) * QCW], ALU.mult))
                        dve_pm_k[kt] = p.now("dve")
                        if kt >= 3:
                            emit_av(kt - 3)
                    emit_av(KT - 3)
                    emit_av(KT - 2)
                    emit_av(KT - 1)
                    keep_mark[kp] = p.now("dve")
                    pe_av_done = p.now("pe")
                    W("vector", "pe", pe_av_done)
                    OP("vector", lambda e, zb=zb: e.reciprocal(srow[0:1, :], ps[zb][0:1, 0:QCW]))
                    dve_srow = p.now("dve")
                    W("tensor", "dve", dve_srow)
                    W("tensor", "dve", zb_mark[(qc + 1) % 2])
                    OP("tensor", lambda e, ob=ob: e.matmul(
                        ps[ob][:, 0:QCW], onesr[0:1, :], srow[0:1, :],
                        start=True, stop=True), inc=("pe", 1))
                    pe_bc = p.now("pe")
                    dcb = h // 2
                    W("scalar", "pe", pe_bc)
                    W("scalar", "dve", bcS_free[0])
                    OP("scalar", lambda e, ob=ob: e.copy(bcS[64:128, :], ps[ob][64:128, 0:QCW]),
                       inc=("act", 1))
                    act_bc = p.now("act")
                    W("vector", "act", act_bc)
                    if h % 2 == 1:
                        OP("vector", lambda e, zb=zb, dcb=dcb, qc=qc: e.tensor_tensor(
                            tm[64:128, dcb * S + qc * QCW:dcb * S + (qc + 1) * QCW],
                            ps[zb][64:128, 0:QCW], bcS[64:128, :], ALU.mult))
                    else:
                        W("vector", "qa", zst_free[0])
                        OP("vector", lambda e, zb=zb: e.tensor_tensor(
                            zstage[64:128, :], ps[zb][64:128, 0:QCW], bcS[64:128, :],
                            ALU.mult))
                        W("scalar", "dve", p.now("dve"))
                        OP("scalar", lambda e, dcb=dcb, qc=qc: e.dma_start(
                            tm[0:64, dcb * S + qc * QCW:dcb * S + (qc + 1) * QCW],
                            zstage[64:128, :]), inc=("qa", 16))
                        zst_free[0] = p.now("qa")
                    bcS_free[0] = p.now("dve")
                    zb_mark[qc % 2] = p.now("dve")
                    zb_mark[(qc + 1) % 2] = p.now("dve")
                qk_evac_guard = p.now("pe")
                wq_mark[hp] = p.now("pe")

            if "z_all" in tap_d:
                p.barrier()
                OP("sync", lambda e: e.dma_start(tap_d["z_all"], tm[:, :]), inc=("qs", 16))
                tap_flush()

        psp_flush()
        if go("B2"):
            p.barrier()
            woS2 = X[:, 2048:10240]
            for doc in range(2):
                OP("sync", lambda e, doc=doc: e.dma_start(
                    woS2[:, doc * 4096:(doc + 1) * 4096].rearrange("p (dc w) -> p dc w", w=512),
                    woT_d.rearrange("(dc p) (dd w) -> p dc dd w", p=128, w=512)[:, :, doc, :]),
                   inc=("qs", 16))
            qs_wo = p.now("qs")
            resb_mark = [0, 0]
            b2_act = [0] * TT
            b2_dve = [0] * TT

            def b2_res(t):
                pr = t % 2
                W("scalar", "dve", xt_mark[pr])
                OP("scalar", lambda e, t=t, pr=pr: e.dma_start(xt[pr], x_d[t * 128:(t + 1) * 128, :]),
                   inc=("qa", 16))
                qa_x = p.now("qa")
                W("vector", "qa", qa_x)
                OP("vector", lambda e, t=t, pr=pr: e.tensor_scalar(
                    lntile[:, :], xt[pr][:, :], m1[:, t:t + 1], r1[:, t:t + 1],
                    ALU.subtract, ALU.mult))
                OP("vector", lambda e, t=t, pr=pr: e.tensor_tensor(
                    fm[:, t * D:(t + 1) * D], xt[pr][:, :], lntile[:, :], ALU.add))
                xt_mark[pr] = p.now("dve")
                b2_dve[t] = p.now("dve")

            def b2_wo(t):
                for doc in range(2):
                    b = doc
                    W("tensor", "qs", qs_wo)
                    W("tensor", "dve", resb_mark[b])
                    for dc in range(DC):
                        OP("tensor", lambda e, b=b, dc=dc, t=t, doc=doc: e.matmul(
                            ps[b][:, :], tm[:, dc * S + t * 128:dc * S + (t + 1) * 128],
                            woS2[:, doc * 4096 + dc * 512:doc * 4096 + (dc + 1) * 512],
                            start=(dc == 0), stop=(dc == DC - 1)),
                           inc=("pe", 1) if dc == DC - 1 else None)
                    W("vector", "pe", p.now("pe"))
                    W("vector", "dve", b2_dve[t])
                    OP("vector", lambda e, b=b, t=t, doc=doc: e.tensor_tensor(
                        fm[:, t * D + doc * 512:t * D + (doc + 1) * 512],
                        fm[:, t * D + doc * 512:t * D + (doc + 1) * 512],
                        ps[b][:, :], ALU.add))
                    resb_mark[b] = p.now("dve")

            for t in range(TT):
                b2_res(t)
                if t >= 1:
                    b2_wo(t - 1)
            b2_wo(TT - 1)
            if "res_attn" in tap_d:
                p.barrier()
                OP("sync", lambda e: e.dma_start(tap_d["res_attn"], fm[:, :]), inc=("qs", 16))
                tap_flush()

        if go("CD"):
            p.barrier()
            OP("sync", lambda e: e.dma_start(gwS[:, :].rearrange("p (dc w) -> p dc w", w=E),
                                             gwT_d.rearrange("(dc p) ee -> p dc ee", p=128)),
               inc=("qs", 16))
            OP("sync", lambda e: e.dma_start(ebias_s[:, :], ebias_d), inc=("qs", 16))
            qs_gw = p.now("qs")
            W("vector", "qs", qs_gw)
            OP("vector", lambda e: e.tensor_copy(gwH[:, :], gwS[:, :]))
            OP("vector", lambda e: e.scalar_tensor_tensor(gwL[:, :], gwH[:, :], -1.0,
                                                          gwS[:, :], ALU.mult, ALU.add))
            dve_gw = p.now("dve")
            W("tensor", "qs", qs_gw)
            OP("tensor", lambda e: e.matmul(ps[7][:, 0:E], onesr[0:1, :], ebias_s[0:1, :],
                                            start=True, stop=True), inc=("pe", 1))
            W("vector", "pe", p.now("pe"))
            OP("vector", lambda e: e.tensor_copy(biasbc[:, :], ps[7][:, 0:E]))
            dve_biasbc = p.now("dve")

            gb_mark = [0, 0]
            lnt_mark2 = [0]
            cd_actmark = [0] * TT
            cd_dvemark = [0] * TT

            def emit_cd(t):
                OP("vector", lambda e, t=t: e.bn_stats(bnbuf[:, 0:6], fm[:, t * D:t * D + 512]))
                OP("vector", lambda e, t=t: e.bn_stats(bnbuf[:, 6:12], fm[:, t * D + 512:(t + 1) * D]))
                OP("vector", lambda e: e.bn_aggr(mv[:, :], bnbuf[:, :]))
                W("scalar", "dve", p.now("dve"))
                OP("scalar", lambda e: e.activation(rtile[:, 0:1], mv[:, 1:2], AF.Sqrt, bias=c_eps[:, 0:1]),
                   inc=("act", 1))
                W("vector", "act", p.now("act"))
                OP("vector", lambda e, t=t: e.reciprocal(r2[:, t:t + 1], rtile[:, 0:1]))
                OP("vector", lambda e, t=t: e.tensor_copy(m2[:, t:t + 1], mv[:, 0:1]))
                W("vector", "pe", lnt_mark2[0])
                OP("vector", lambda e, t=t: e.tensor_scalar(
                    lntile[:, :], fm[:, t * D:(t + 1) * D], m2[:, t:t + 1], r2[:, t:t + 1],
                    ALU.subtract, ALU.mult))
                dve_ln2 = p.now("dve")
                for dc in range(DC):
                    b = trb[dc % 2]
                    W("tensor", "act", trbA[dc % 2])
                    W("tensor", "dve", trbV[dc % 2])
                    W("tensor", "dve", dve_ln2)
                    OP("tensor", lambda e, b=b, dc=dc: e.transpose(
                        ps[b][:, 0:128], lntile[:, dc * 128:(dc + 1) * 128], ident[:, :]),
                       inc=("pe", 1))
                    pe_tr = p.now("pe")
                    W("scalar", "pe", pe_tr)
                    OP("scalar", lambda e, b=b, dc=dc, t=t: e.copy(
                        fmH[:, dc * S + t * 128:dc * S + (t + 1) * 128], ps[b][:, 0:128]),
                       inc=("act", 1))
                    trbA[dc % 2] = p.now("act")
                    W("vector", "pe", pe_tr)
                    W("vector", "act", trbA[dc % 2])
                    OP("vector", lambda e, b=b, dc=dc, t=t: e.scalar_tensor_tensor(
                        fmL[:, dc * S + t * 128:dc * S + (t + 1) * 128],
                        fmH[:, dc * S + t * 128:dc * S + (t + 1) * 128], -1.0,
                        ps[b][:, 0:128], ALU.mult, ALU.add))
                    trbV[dc % 2] = p.now("dve")
                lnt_mark2[0] = p.now("pe")
                act_ln2T_t = p.now("act")
                dve_ln2T_t = p.now("dve")
                b = 2 + (t % 2)
                W("tensor", "qs", qs_gw)
                W("tensor", "act", act_ln2T_t)
                W("tensor", "dve", dve_ln2T_t)
                W("tensor", "dve", dve_gw)
                W("tensor", "act", gb_mark[t % 2])
                np_ = 0
                for lA, lB in ((fmH, gwH), (fmH, gwL), (fmL, gwH)):
                    for dc in range(DC):
                        np_ += 1
                        OP("tensor", lambda e, b=b, dc=dc, t=t, lA=lA, lB=lB, np_=np_: e.matmul(
                            ps[b][:, 0:E], lA[:, dc * S + t * 128:dc * S + (t + 1) * 128],
                            lB[:, dc * E:(dc + 1) * E], start=(np_ == 1), stop=(np_ == 3 * DC)),
                           inc=("pe", 1) if np_ == 3 * DC else None)
                W("scalar", "pe", p.now("pe"))
                OP("scalar", lambda e, b=b: e.activation(sctile[:, :], ps[b][:, 0:E], AF.Sigmoid),
                   inc=("act", 1))
                gb_mark[t % 2] = p.now("act")
                W("vector", "act", p.now("act"))
                W("vector", "dve", dve_biasbc)
                OP("vector", lambda e: e.tensor_tensor(seltile[:, :], sctile[:, :], biasbc[:, :], ALU.add))
                OP("vector", lambda e: e.tensor_copy(work[:, :], seltile[:, :]))
                for it in range(TOPK):
                    OP("vector", lambda e: e.reduce_max(t8[:, :], work[:, :], axis=AX.X))
                    if it < TOPK - 1:
                        OP("vector", lambda e: e.tensor_scalar(eqt[:, :], work[:, :], t8[:, 0:1], None,
                                                               ALU.is_equal))
                        OP("vector", lambda e: e.scalar_tensor_tensor(work[:, :], eqt[:, :], -1e9,
                                                                      work[:, :], ALU.mult, ALU.add))
                OP("vector", lambda e: e.tensor_scalar(eqt[:, :], seltile[:, :], t8[:, 0:1], None,
                                                       ALU.is_ge))
                OP("vector", lambda e: e.tensor_tensor(work[:, :], sctile[:, :], eqt[:, :], ALU.mult))
                OP("vector", lambda e: e.reduce_sum(s8[:, :], work[:, :], axis=AX.X))
                OP("vector", lambda e: e.reciprocal(s8[:, :], s8[:, :]))
                OP("vector", lambda e, t=t: e.tensor_scalar(
                    comb[:, t * E:(t + 1) * E], work[:, :], s8[:, 0:1], ROUTE_SCALE,
                    ALU.mult, ALU.mult))
                cd_actmark[t] = p.now("act")
                cd_dvemark[t] = p.now("dve")

            if not go("E"):
                for _t in range(TT):
                    emit_cd(_t)
            if "ln2T" in tap_d:
                p.barrier()
                OP("sync", lambda e: e.dma_start(tap_d["ln2T"], tm[:, :]), inc=("qs", 16))
                tap_flush()
            if "combine" in tap_d:
                p.barrier()
                OP("sync", lambda e: e.dma_start(tap_d["combine"], comb[:, :]), inc=("qs", 16))
                tap_flush()

        if go("E"):
            W("sync", "pe")
            W("sync", "dve")
            _pe0 = p.now("pe")
            wg_mark = [_pe0, _pe0]
            wu_mark = [_pe0, _pe0]
            wd_mark = [_pe0, _pe0]
            hb_mark = [[0, 0], [0, 0]]
            htldF_mark = [0]
            yb_mark = [0] * 8

            for e_i in range(n_experts):
                shared = (e_i == E)
                for tc in range(NTC):
                    if e_i == 0:
                        W("tensor", "act")
                        W("tensor", "dve")
                        groups = [0, 1] if tc == 0 else ([tc + 1] if tc + 1 < NTC else [])
                        for g in groups:
                            for _t in range(g * 4, g * 4 + 4):
                                emit_cd(_t)
                    W("tensor", "act", cd_actmark[tc * 4 + 3])
                    W("tensor", "dve", cd_dvemark[tc * 4 + 3])
                    for ic in range(DC):
                        wp = ic % 2
                        wgi = (e_i * 3 + 0) * DC + ic
                        wui = (e_i * 3 + 1) * DC + ic
                        W("sync", "pe", wg_mark[wp])
                        W("sync", "qg", wsp_ready[wgi])
                        OP("sync", lambda e, wgi=wgi, wp=wp: e.dma_start(wgHL[wp][:, :], wsp_d[wgi]),
                           inc=("qs", 16))
                        W("sync", "pe", wu_mark[wp])
                        W("sync", "qg", wsp_ready[wui])
                        OP("sync", lambda e, wui=wui, wp=wp: e.dma_start(wuHL[wp][:, :], wsp_d[wui]),
                           inc=("qs", 16))
                        qs_w = p.now("qs")
                        gb0 = 2 * (ic % 2)
                        gb1 = gb0 + 1
                        W("tensor", "qs", qs_w)
                        W("tensor", "act", hb_mark[ic % 2][0])
                        W("tensor", "dve", hb_mark[ic % 2][1])
                        if ic < 2:
                            for bb in range(4):
                                W("tensor", "dve", yb_mark[bb])
                        np_ = 0
                        for lA, lB in ((wgH, fmH), (wgL, fmH), (wgH, fmL)):
                            for dc in range(DC):
                                np_ += 1
                                OP("tensor", lambda e, dc=dc, tc=tc, wp=wp, lA=lA, lB=lB, np_=np_, gb0=gb0: e.matmul(
                                    ps[gb0][:, :], lA[wp][:, dc * 128:(dc + 1) * 128],
                                    lB[:, dc * S + tc * TCW:dc * S + (tc + 1) * TCW],
                                    start=(np_ == 1), stop=(np_ == 3 * DC)),
                                   inc=("pe", 1) if np_ == 3 * DC else None)
                        np_ = 0
                        for lA, lB in ((wuH, fmH), (wuL, fmH), (wuH, fmL)):
                            for dc in range(DC):
                                np_ += 1
                                OP("tensor", lambda e, dc=dc, tc=tc, wp=wp, lA=lA, lB=lB, np_=np_, gb1=gb1: e.matmul(
                                    ps[gb1][:, :], lA[wp][:, dc * 128:(dc + 1) * 128],
                                    lB[:, dc * S + tc * TCW:dc * S + (tc + 1) * TCW],
                                    start=(np_ == 1), stop=(np_ == 3 * DC)),
                                   inc=("pe", 1) if np_ == 3 * DC else None)
                        wg_mark[wp] = p.now("pe")
                        wu_mark[wp] = p.now("pe")
                        pe_gu = p.now("pe")
                        W("scalar", "pe", pe_gu)
                        W("scalar", "dve", htldF_mark[0])
                        OP("scalar", lambda e, ic=ic, gb0=gb0: e.activation(
                            htldF[:, :], ps[gb0][:, :], AF.Silu), inc=("act", 1))
                        hb_mark[ic % 2][0] = p.now("act")
                        W("vector", "act", p.now("act"))
                        OP("vector", lambda e, ic=ic, gb1=gb1: e.tensor_tensor(
                            htldF[:, :], htldF[:, :], ps[gb1][:, :], ALU.mult))
                        hb_mark[ic % 2][1] = p.now("dve")
                        OP("vector", lambda e, ic=ic: e.tensor_copy(
                            htldH[:, ic * TCW:(ic + 1) * TCW], htldF[:, :]))
                        OP("vector", lambda e, ic=ic: e.scalar_tensor_tensor(
                            htldL[:, ic * TCW:(ic + 1) * TCW], htldH[:, ic * TCW:(ic + 1) * TCW],
                            -1.0, htldF[:, :], ALU.mult, ALU.add))
                        htldF_mark[0] = p.now("dve")
                    dve_h = p.now("dve")
                    if e_i == 0 and tc == 0 and "htld0" in tap_d:
                        p.barrier()
                        OP("sync", lambda e: e.dma_start(tap_d["htld0"],
                                                         htldH[:, :].bitcast(F32)), inc=("qs", 16))
                        tap_flush()
                    for ic in range(DC):
                        wp = ic % 2
                        wdi = (e_i * 3 + 2) * DC + ic
                        W("sync", "pe", wd_mark[wp])
                        W("sync", "qg", wsp_ready[wdi])
                        OP("sync", lambda e, wdi=wdi, wp=wp: e.dma_start(wdHL[wp][:, :], wsp_d[wdi]),
                           inc=("qs", 16))
                        W("tensor", "qs", p.now("qs"))
                        W("tensor", "dve", dve_h)
                        if ic == 0:
                            for bb in range(8):
                                W("tensor", "dve", yb_mark[bb])
                        for pi, (lA, lB) in enumerate(((htldH, wdH), (htldL, wdH), (htldH, wdL))):
                            for ts4 in range(4):
                                for doc in range(2):
                                    bb = ts4 * 2 + doc
                                    OP("tensor", lambda e, bb=bb, ic=ic, ts4=ts4, doc=doc, wp=wp,
                                       lA=lA, lB=lB, pi=pi: e.matmul(
                                        ps[bb][:, :], lA[:, ic * TCW + ts4 * 128:ic * TCW + (ts4 + 1) * 128],
                                        lB[wp][:, doc * 512:(doc + 1) * 512],
                                        start=(ic == 0 and pi == 0), stop=(ic == DC - 1 and pi == 2)),
                                       inc=("pe", 1) if bb == 7 and pi == 2 else None)
                        wd_mark[wp] = p.now("pe")
                    pe_down = p.now("pe")
                    for ts4 in range(4):
                        for doc in range(2):
                            bb = ts4 * 2 + doc
                            tt_i = tc * 4 + ts4
                            W("vector", "pe", pe_down)
                            if shared:
                                OP("vector", lambda e, bb=bb, tt_i=tt_i, doc=doc: e.tensor_tensor(
                                    fm[:, tt_i * D + doc * 512:tt_i * D + (doc + 1) * 512],
                                    fm[:, tt_i * D + doc * 512:tt_i * D + (doc + 1) * 512],
                                    ps[bb][:, :], ALU.add))
                            else:
                                OP("vector", lambda e, bb=bb, tt_i=tt_i, doc=doc, e_i=e_i:
                                   e.scalar_tensor_tensor(
                                       fm[:, tt_i * D + doc * 512:tt_i * D + (doc + 1) * 512],
                                       ps[bb][:, :], comb[:, tt_i * E + e_i:tt_i * E + e_i + 1],
                                       fm[:, tt_i * D + doc * 512:tt_i * D + (doc + 1) * 512],
                                       ALU.mult, ALU.add))
                            yb_mark[bb] = p.now("dve")
            if "res_ffn" in tap_d:
                p.barrier()
                OP("sync", lambda e: e.dma_start(tap_d["res_ffn"], fm[:, :]), inc=("qs", 16))
                tap_flush()

        if stop is None:
            p.barrier()
            for t in range(TT):
                OP("sync", lambda e, t=t: e.dma_start(out_d[t * 128:(t + 1) * 128, :],
                                                      fm[:, t * D:(t + 1) * D]), inc=("qs", 16))
            p.wait("sync", "qs", p.now("qs"))

        def make_body(ename):
            def body(eng):
                for fn, inc in p.q[ename]:
                    if isinstance(fn, tuple) and fn[0] == "wait":
                        eng.wait_ge(sems[fn[1]], fn[2])
                    else:
                        r = fn(eng)
                        if inc is not None:
                            r.then_inc(sems[inc[0]], inc[1])
            return body

        for ename in ENGS:
            getattr(block, ename)(make_body(ename))

    return nc


def _stage(inputs):
    f32 = np.float32
    minds = np.asarray(inputs["L0_attn_mask_inds"], np.int64)
    q = minds[0]
    k = minds[1]
    reg = (q // QCW).astype(np.int64)
    order = np.argsort(reg, kind="stable")
    qs_ = q[order]
    ks_ = k[order]
    regs = reg[order]
    q2 = np.zeros((MREG, MCAP), np.int32)
    k2 = np.full((MREG, MCAP), S, np.int32)
    for r in range(MREG):
        sel = regs == r
        n = int(sel.sum())
        assert n <= MCAP, f"mask region {r} overflow: {n}"
        q2[r, :n] = qs_[sel]
        k2[r, :n] = ks_[sel]
    q3 = q2.reshape(MREG, 128, RB).transpose(1, 0, 2).reshape(-1)
    k3 = k2.reshape(MREG, 128, RB).transpose(1, 0, 2).reshape(-1)
    minds2 = np.ascontiguousarray(np.stack([q3, k3]).astype(np.int32))

    base = {
        "minds": minds2,
        "wqT": np.ascontiguousarray(np.asarray(inputs["L0_attn_Wq"], f32).T),
        "wkT": np.ascontiguousarray(np.asarray(inputs["L0_attn_Wk"], f32).T),
        "wvT": np.ascontiguousarray(np.asarray(inputs["L0_attn_Wv"], f32).T),
        "woT": np.ascontiguousarray(np.asarray(inputs["L0_attn_Wo"], f32).T),
        "gwT": np.ascontiguousarray(np.asarray(inputs["L0_ffn_gate_w"], f32).T),
        "ebias": np.ascontiguousarray(np.asarray(inputs["L0_ffn_expert_bias"], f32).reshape(1, E)),
        "wgT": np.ascontiguousarray(np.asarray(inputs["L0_ffn_w_gate"], f32).transpose(0, 2, 1)),
        "wuT": np.ascontiguousarray(np.asarray(inputs["L0_ffn_w_up"], f32).transpose(0, 2, 1)),
        "wdT": np.ascontiguousarray(np.asarray(inputs["L0_ffn_w_down"], f32).transpose(0, 2, 1)),
        "sgT": np.ascontiguousarray(np.asarray(inputs["L0_ffn_shared_gate_w"], f32).T),
        "suT": np.ascontiguousarray(np.asarray(inputs["L0_ffn_shared_up_w"], f32).T),
        "sdT": np.ascontiguousarray(np.asarray(inputs["L0_ffn_shared_down_w"], f32).T),
    }
    x = np.asarray(inputs["L0_attn_x"], f32)
    return [dict(base, x=np.ascontiguousarray(x[b])) for b in range(8)]


def kernel(**inputs):
    in_maps = _stage(inputs)
    nc = build()
    r = run_bass_kernel_spmd(nc, in_maps, core_ids=list(range(8)))
    return np.stack([np.asarray(m["out"], np.float32) for m in r.results], axis=0)


# version 17 (final)
# speedup vs baseline: 1.0564x; 1.0564x over previous
import numpy as np
from contextlib import ExitStack

import concourse.bass as bass
import concourse.mybir as mybir
from concourse.bass_utils import run_bass_kernel_spmd
from concourse.bass import IndirectOffsetOnAxis

F32 = mybir.dt.float32
BF16 = mybir.dt.bfloat16
FP8 = mybir.dt.float8e4
I32 = mybir.dt.int32
U32 = mybir.dt.uint32
AF = mybir.ActivationFunctionType
ALU = mybir.AluOpType
AX = mybir.AxisListType

S = 2048
D = 1024
H = 16
DH = 64
E = 16
TT = 16
DC = 8
QCW = 512
NQC = S // QCW
NQT = QCW // 128
KT = 16
NMI = 100000
MREG = NQC
MCAP = 26624
NMI2 = MREG * MCAP
MIB = NMI2 // 128
RB = MIB // MREG
SCALE = 0.125
ROUTE_SCALE = 2.826
EPS = 1e-5
TOPK = 8
TCW = 512
NTC = S // TCW

ENGS = ("sync", "tensor", "vector", "scalar", "gpsimd")
PROD = {"pe": "tensor", "dve": "vector", "act": "scalar", "pool": "gpsimd"}


class Prog:
    SELFCHAIN = {"vector": "dve"}

    def __init__(self):
        self.q = {e: [] for e in ENGS}
        self.cnt = {}
        self.lastwait = {e: {} for e in ENGS}

    def op(self, eng, fn, inc=None):
        sem = self.SELFCHAIN.get(eng)
        if sem is not None:
            cur = self.cnt.get(sem, 0)
            if cur > 0 and self.lastwait[eng].get(sem, 0) < cur:
                self.q[eng].append((("wait", sem, cur), None))
                self.lastwait[eng][sem] = cur
            if inc is None:
                inc = (sem, 1)
            else:
                assert inc[0] == sem, (eng, inc)
        self.q[eng].append((fn, inc))
        if inc:
            self.cnt[inc[0]] = self.cnt.get(inc[0], 0) + inc[1]
        return self.cnt.get(inc[0], 0) if inc else None

    def now(self, sem):
        return self.cnt.get(sem, 0)

    def wait(self, eng, sem, val=None):
        if val is None:
            val = self.cnt.get(sem, 0)
        if val <= 0:
            return
        if PROD.get(sem) == eng:
            return
        if self.lastwait[eng].get(sem, 0) >= val:
            return
        self.lastwait[eng][sem] = val
        self.q[eng].append((("wait", sem, val), None))

    def barrier(self):
        for e in ENGS:
            for s in list(self.cnt):
                self.wait(e, s)


def build(taps=(), stop=None, n_experts=E + 1):
    nc = bass.Bass("TRN2", target_bir_lowering=False, detect_race_conditions=False)

    x_d = nc.dram_tensor("x", [S, D], F32, kind="ExternalInput").ap()
    minds_d = nc.dram_tensor("minds", [2, NMI2], I32, kind="ExternalInput").ap()
    wqT_d = nc.dram_tensor("wqT", [D, D], F32, kind="ExternalInput").ap()
    wkT_d = nc.dram_tensor("wkT", [D, D], F32, kind="ExternalInput").ap()
    wvT_d = nc.dram_tensor("wvT", [D, D], F32, kind="ExternalInput").ap()
    woT_d = nc.dram_tensor("woT", [D, D], F32, kind="ExternalInput").ap()
    gwT_d = nc.dram_tensor("gwT", [D, E], F32, kind="ExternalInput").ap()
    ebias_d = nc.dram_tensor("ebias", [1, E], F32, kind="ExternalInput").ap()
    wgT_d = nc.dram_tensor("wgT", [E, D, D], F32, kind="ExternalInput").ap()
    wuT_d = nc.dram_tensor("wuT", [E, D, D], F32, kind="ExternalInput").ap()
    wdT_d = nc.dram_tensor("wdT", [E, D, D], F32, kind="ExternalInput").ap()
    sgT_d = nc.dram_tensor("sgT", [D, D], F32, kind="ExternalInput").ap()
    suT_d = nc.dram_tensor("suT", [D, D], F32, kind="ExternalInput").ap()
    sdT_d = nc.dram_tensor("sdT", [D, D], F32, kind="ExternalInput").ap()
    out_d = nc.dram_tensor("out", [S, D], F32, kind="ExternalOutput").ap()
    mask_d = nc.dram_tensor("mask_hbm", [S + 1, S], FP8, kind="Internal").ap()
    wsp_d = nc.dram_tensor("wsp_hbm", [(E + 1) * 3 * DC, 128, 2048], BF16, kind="Internal").ap()
    tap_d = {}
    for name, shape in dict(
        ln1T=[128, DC * S], z_all=[128, TT * D], res_attn=[128, TT * D],
        ln2T=[128, DC * S], combine=[128, TT * E], q0=[64, S], v0=[128, TT * 65],
        htld0=[128, DC * TCW // 2], res_ffn=[128, TT * D],
    ).items():
        if name in taps:
            tap_d[name] = nc.dram_tensor("tap_" + name, shape, F32, kind="ExternalOutput").ap()

    p = Prog()

    with ExitStack() as ctx:
        def sbuf(name, shape, dtype=F32):
            return ctx.enter_context(nc.sbuf_tensor(name, shape, dtype))

        fm = sbuf("fm", [128, DC * S])
        tm = sbuf("tm", [128, TT * D])
        fmH = tm[:, 0:TT * D // 2].bitcast(BF16)
        fmL = tm[:, TT * D // 2:TT * D].bitcast(BF16)
        X = sbuf("X", [128, 12800])
        xt = [X[:, i * 1024:(i + 1) * 1024] for i in range(2)]
        woS = X[:, 2048:2048 + 4096]
        qT = X[:, 0:2048]
        kT_ = X[:, 2048:4096]
        qTH = X[:, 0:1024].bitcast(BF16)
        qTL = X[:, 1024:2048].bitcast(BF16)
        kTH = X[:, 2048:3072].bitcast(BF16)
        kTL = X[:, 3072:4096].bitcast(BF16)
        vA = X[:, 4096:6144]
        wqS = [X[:, 6144 + i * 512:6144 + (i + 1) * 512] for i in range(2)]
        wkS = [X[:, 7168 + i * 512:7168 + (i + 1) * 512] for i in range(2)]
        wvS = [X[:, 8192 + i * 512:8192 + (i + 1) * 512] for i in range(2)]
        zstage = X[:, 11776:12288]
        srow = X[0:1, 12288:12800]
        bcS = X[:, 12288:12800]
        wgHL = [X[:, i * 1024:(i + 1) * 1024].bitcast(BF16) for i in range(2)]
        wuHL = [X[:, 2048 + i * 1024:2048 + (i + 1) * 1024].bitcast(BF16) for i in range(2)]
        wdHL = [X[:, 4096 + i * 1024:4096 + (i + 1) * 1024].bitcast(BF16) for i in range(2)]
        wgH = [w[:, 0:1024] for w in wgHL]
        wgL = [w[:, 1024:2048] for w in wgHL]
        wuH = [w[:, 0:1024] for w in wuHL]
        wuL = [w[:, 1024:2048] for w in wuHL]
        wdH = [w[:, 0:1024] for w in wdHL]
        wdL = [w[:, 1024:2048] for w in wdHL]
        htldF = X[:, 6144:6656]
        htldH = X[:, 6656:8704].bitcast(BF16)
        htldL = X[:, 8704:10752].bitcast(BF16)
        ps_stage = X[:, 9216:10240]
        ps_hilo = X[:, 10240:11264].bitcast(BF16)

        pbuf = sbuf("pbuf", [128, 4 * QCW])
        keep8 = sbuf("keep8", [128, 2 * KT * QCW], FP8)
        offs = sbuf("offs", [128, MIB], U32)
        offf = X[:, 2048:2880]
        idxf = X[:, 2880:3712]
        ones8 = X[:, 6144:6656].bitcast(FP8)
        zero8 = sbuf("zero8", [128, 1], FP8)
        ident = sbuf("ident", [128, 128])
        onesr = sbuf("onesr", [1, 128])
        bnbuf = sbuf("bnbuf", [128, 12])
        mv = sbuf("mv", [128, 2])
        m1 = sbuf("m1", [128, TT])
        r1 = sbuf("r1", [128, TT])
        m2 = sbuf("m2", [128, TT])
        r2 = sbuf("r2", [128, TT])
        lntile = X[:, 11776:12800]
        rtile = sbuf("rtile", [128, 2])
        ebias_s = sbuf("ebias_s", [1, E])
        c_eps = sbuf("c_eps", [128, 1])
        biasbc = sbuf("biasbc", [128, E])
        sctile = sbuf("sctile", [128, E])
        seltile = sbuf("seltile", [128, E])
        work = sbuf("work", [128, E])
        eqt = sbuf("eqt", [128, E])
        t8 = sbuf("t8", [128, 1])
        s8 = sbuf("s8", [128, 1])
        comb = X[:, 11264:11520]
        gwS = X[:, 11520:11648]
        gwH = X[:, 11648:11712].bitcast(BF16)
        gwL = X[:, 11712:11776].bitcast(BF16)

        ps = [ctx.enter_context(nc.psum_tensor(f"ps{i}", [128, 512], F32)) for i in range(8)]

        sems = {}
        for sname in ("qs", "qa", "qg", "pe", "dve", "act", "pool", "sc"):
            sems[sname] = ctx.enter_context(nc.semaphore("sem_" + sname))

        block = ctx.enter_context(nc.Block())

        def OP(eng, fn, inc=None):
            return p.op(eng, fn, inc)

        def W(eng, sem, val=None):
            p.wait(eng, sem, val)

        def tap_flush():
            for _e in ENGS:
                p.wait(_e, "qs")

        STOP_ORDER = ["A", "B", "B2", "CD", "E", None]

        def go(tag):
            if stop is None:
                return True
            return STOP_ORDER.index(tag) <= STOP_ORDER.index(stop)

        OP("gpsimd", lambda e: e.memset(ones8[:, :], 1.0))
        OP("gpsimd", lambda e: e.memset(zero8[:, :], 0.0))
        OP("gpsimd", lambda e: e.memset(onesr[:, :], 1.0))
        OP("gpsimd", lambda e: e.memset(c_eps[:, :], EPS))
        OP("gpsimd", lambda e: e.memset(vA[:, :], 0.0))
        for t in range(TT):
            OP("gpsimd", lambda e, t=t: e.memset(vA[:, t * 128:t * 128 + 1], 1.0))
        OP("gpsimd", lambda e: e.iota(ident[:, :], [[-1, 128]], base=0, channel_multiplier=1,
                                      allow_small_or_imprecise_dtypes=True), inc=("pool", 1))
        pool_iota = p.now("pool")
        W("vector", "pool", pool_iota)
        OP("vector", lambda e: e.tensor_scalar(ident[:, :], ident[:, :], 0.0, None, ALU.is_equal))
        dve_ident = p.now("dve")

        OP("sync", lambda e: e.dma_start(
            offf[:, :].bitcast(I32),
            minds_d[0:1, :].rearrange("o (pp c) -> (o pp) c", pp=128)), inc=("qs", 16))
        OP("sync", lambda e: e.dma_start(
            idxf[:, :].bitcast(I32),
            minds_d[1:2, :].rearrange("o (pp c) -> (o pp) c", pp=128)), inc=("qs", 16))
        qs_minds = p.now("qs")

        W("scalar", "pool", pool_iota)
        for i in range(TT):
            OP("scalar", lambda e, i=i: e.dma_start(mask_d[i * 128:(i + 1) * 128, :], ones8[:, :]),
               inc=("qa", 16))
        OP("scalar", lambda e: e.dma_start(mask_d[S:S + 1, :], ones8[0:1, :]), inc=("qa", 16))
        qa_maskinit = p.now("qa")

        W("vector", "qs", qs_minds)
        OP("vector", lambda e: e.tensor_copy(offf[:, :], offf[:, :].bitcast(I32)))
        OP("vector", lambda e: e.tensor_copy(idxf[:, :], idxf[:, :].bitcast(I32)))
        OP("vector", lambda e: e.scalar_tensor_tensor(offf[:, :], idxf[:, :], float(S),
                                                      offf[:, :], ALU.mult, ALU.add))
        OP("vector", lambda e: e.tensor_copy(offs[:, :], offf[:, :]))
        dve_offs = p.now("dve")

        W("gpsimd", "dve", dve_offs)
        W("gpsimd", "qa", qa_maskinit)
        sc_thr = []
        for c in range(MIB):
            OP("gpsimd", lambda e, c=c: e.indirect_dma_start(
                mask_d.flatten().unsqueeze(1),
                IndirectOffsetOnAxis(ap=offs[:, c:c + 1], axis=0),
                zero8[:, 0:1], None), inc=("sc", 16))
            if (c + 1) % RB == 0:
                sc_thr.append(p.now("sc"))
        assert len(sc_thr) == MREG

        xt_mark = [0, 0]
        trb = [5, 6]
        trbA = [0, 0]
        trbV = [0, 0]
        lnt_mark = 0
        for t in range(TT):
            pr = t % 2
            W("scalar", "dve", xt_mark[pr])
            OP("scalar", lambda e, t=t, pr=pr: e.dma_start(xt[pr], x_d[t * 128:(t + 1) * 128, :]),
               inc=("qa", 16))
            qa_x = p.now("qa")
            W("vector", "qa", qa_x)
            OP("vector", lambda e, pr=pr: e.bn_stats(bnbuf[:, 0:6], xt[pr][:, 0:512]))
            OP("vector", lambda e, pr=pr: e.bn_stats(bnbuf[:, 6:12], xt[pr][:, 512:1024]))
            OP("vector", lambda e: e.bn_aggr(mv[:, :], bnbuf[:, :]))
            W("scalar", "dve", p.now("dve"))
            OP("scalar", lambda e: e.activation(rtile[:, 0:1], mv[:, 1:2], AF.Sqrt, bias=c_eps[:, 0:1]),
               inc=("act", 1))
            W("vector", "act", p.now("act"))
            OP("vector", lambda e, t=t: e.reciprocal(r1[:, t:t + 1], rtile[:, 0:1]))
            OP("vector", lambda e, t=t: e.tensor_copy(m1[:, t:t + 1], mv[:, 0:1]))
            W("vector", "pe", lnt_mark)
            OP("vector", lambda e, t=t, pr=pr: e.tensor_scalar(
                lntile[:, :], xt[pr][:, :], m1[:, t:t + 1], r1[:, t:t + 1],
                ALU.subtract, ALU.mult))
            xt_mark[pr] = p.now("dve")
            dve_ln = p.now("dve")
            for dc in range(DC):
                b = trb[dc % 2]
                W("tensor", "act", trbA[dc % 2])
                W("tensor", "dve", dve_ln)
                W("tensor", "dve", dve_ident)
                W("tensor", "pool", pool_iota)
                OP("tensor", lambda e, b=b, dc=dc: e.transpose(
                    ps[b][:, 0:128], lntile[:, dc * 128:(dc + 1) * 128], ident[:, :]),
                   inc=("pe", 1))
                W("scalar", "pe", p.now("pe"))
                OP("scalar", lambda e, b=b, dc=dc, t=t: e.copy(
                    fm[:, dc * S + t * 128: dc * S + (t + 1) * 128], ps[b][:, 0:128]),
                   inc=("act", 1))
                trbA[dc % 2] = p.now("act")
            lnt_mark = p.now("pe")
        act_ln1T_done = p.now("act")
        if "ln1T" in tap_d:
            p.barrier()
            OP("sync", lambda e: e.dma_start(tap_d["ln1T"], fm[:, :]), inc=("qs", 16))
            tap_flush()

        psp_steps = []
        wsp_ready = {}
        psp_state = {"stage_free": None, "hilo_free": 0}

        def _psp_emit(widx, kind, e_i, ic):
            def load():
                if kind == "g":
                    srcs = sgT_d if e_i == E else wgT_d[e_i]
                elif kind == "u":
                    srcs = suT_d if e_i == E else wuT_d[e_i]
                else:
                    srcs = sdT_d if e_i == E else wdT_d[e_i]
                if kind == "d":
                    OP("gpsimd", lambda e, srcs=srcs, ic=ic: e.dma_start(
                        ps_stage[:, :], srcs[ic * 128:(ic + 1) * 128, :]), inc=("qg", 16))
                else:
                    OP("gpsimd", lambda e, srcs=srcs, ic=ic: e.dma_start(
                        ps_stage[:, :].rearrange("p (dc w) -> p dc w", w=128),
                        srcs.rearrange("(dc p) (ic w) -> p dc ic w", p=128, w=128)[:, :, ic, :]),
                       inc=("qg", 16))
                W("gpsimd", "qg", p.now("qg"))
                OP("gpsimd", lambda e: e.tensor_copy(ps_hilo[:, 0:1024], ps_stage[:, :]))
                OP("gpsimd", lambda e: e.tensor_tensor(
                    ps_hilo[:, 1024:2048], ps_stage[:, :], ps_hilo[:, 0:1024],
                    ALU.subtract), inc=("pool", 1))
                OP("gpsimd", lambda e, widx=widx: e.dma_start(wsp_d[widx], ps_hilo[:, :]),
                   inc=("qg", 16))
                psp_state["hilo_free"] = p.now("qg")
                wsp_ready[widx] = p.now("qg")
            return load

        _order = []
        for _e in range(E + 1):
            for _ic in range(DC):
                _order.append(("g", _e, _ic))
                _order.append(("u", _e, _ic))
            for _ic in range(DC):
                _order.append(("d", _e, _ic))
        for _kind, _e, _ic in _order:
            _w = {"g": 0, "u": 1, "d": 2}[_kind]
            _widx = (_e * 3 + _w) * DC + _ic
            psp_steps.append(_psp_emit(_widx, _kind, _e, _ic))
        psp_pos = [0]

        def psp_step(n):
            while psp_pos[0] < len(psp_steps) and n > 0:
                psp_steps[psp_pos[0]]()
                psp_pos[0] += 1
                n -= 1

        def psp_flush():
            psp_step(len(psp_steps))

        if go("B"):
            wq_mark = [0, 0]
            keep_mark = [0, 0]
            pb_mark = [0, 0, 0, 0]
            scb_mark = [0, 0, 0, 0]
            zb_mark = [0, 0]
            projb_mark = [0, 0]
            projbV_mark = [0, 0]
            zst_free = [0]
            bcS_free = [0]
            ob_mark = [0]

            for h in range(H):
                hp = h % 2
                W("sync", "pe", wq_mark[hp])
                if h == 0:
                    W("sync", "qa", qa_maskinit)
                OP("sync", lambda e, h=h, hp=hp: e.dma_start(
                    wqS[hp][:, :].rearrange("p (dc w) -> p dc w", w=64),
                    wqT_d.rearrange("(dc p) (hh w) -> p dc hh w", p=128, w=64)[:, :, h, :]),
                   inc=("qs", 16))
                OP("sync", lambda e, h=h, hp=hp: e.dma_start(
                    wkS[hp][:, :].rearrange("p (dc w) -> p dc w", w=64),
                    wkT_d.rearrange("(dc p) (hh w) -> p dc hh w", p=128, w=64)[:, :, h, :]),
                   inc=("qs", 16))
                OP("sync", lambda e, h=h, hp=hp: e.dma_start(
                    wvS[hp][:, :].rearrange("p (dc w) -> p dc w", w=64),
                    wvT_d.rearrange("(dc p) (hh w) -> p dc hh w", p=128, w=64)[:, :, h, :]),
                   inc=("qs", 16))
                qs_w = p.now("qs")

                for which, wS, dstH, dstL in (("q", wqS, qTH, qTL), ("k", wkS, kTH, kTL)):
                    for qc4 in range(4):
                        bpar = qc4 % 2
                        b = 5 + bpar
                        W("tensor", "qs", qs_w)
                        W("tensor", "act", act_ln1T_done)
                        W("tensor", "act", projb_mark[bpar])
                        W("tensor", "dve", projbV_mark[bpar])
                        for dc in range(DC):
                            OP("tensor", lambda e, b=b, dc=dc, qc4=qc4, wS=wS, hp=hp: e.matmul(
                                ps[b][0:64, :], wS[hp][:, dc * 64:(dc + 1) * 64],
                                fm[:, dc * S + qc4 * 512:dc * S + (qc4 + 1) * 512],
                                start=(dc == 0), stop=(dc == DC - 1)),
                               inc=("pe", 1) if dc == DC - 1 else None)
                        pe_proj = p.now("pe")
                        W("scalar", "pe", pe_proj)
                        if h > 0:
                            W("scalar", "pe", qk_evac_guard)
                        OP("scalar", lambda e, b=b, qc4=qc4, dstH=dstH: e.copy(
                            dstH[0:64, qc4 * 512:(qc4 + 1) * 512], ps[b][0:64, :]), inc=("act", 1))
                        projb_mark[bpar] = p.now("act")
                        W("vector", "pe", pe_proj)
                        W("vector", "act", projb_mark[bpar])
                        OP("vector", lambda e, b=b, qc4=qc4, dstH=dstH, dstL=dstL:
                           e.scalar_tensor_tensor(
                               dstL[0:64, qc4 * 512:(qc4 + 1) * 512],
                               dstH[0:64, qc4 * 512:(qc4 + 1) * 512], -1.0,
                               ps[b][0:64, :], ALU.mult, ALU.add))
                        projbV_mark[bpar] = p.now("dve")
                for t in range(TT):
                    bpar = t % 2
                    b = 5 + bpar
                    W("tensor", "act", projb_mark[bpar])
                    W("tensor", "dve", projbV_mark[bpar])
                    for dc in range(DC):
                        OP("tensor", lambda e, b=b, dc=dc, t=t, hp=hp: e.matmul(
                            ps[b][:, 0:64], fm[:, dc * S + t * 128:dc * S + (t + 1) * 128],
                            wvS[hp][:, dc * 64:(dc + 1) * 64],
                            start=(dc == 0), stop=(dc == DC - 1)),
                           inc=("pe", 1) if dc == DC - 1 else None)
                    W("scalar", "pe", p.now("pe"))
                    OP("scalar", lambda e, b=b, t=t: e.copy(
                        vA[:, t * 128 + 64:t * 128 + 128], ps[b][:, 0:64]), inc=("act", 1))
                    projb_mark[bpar] = p.now("act")
                act_proj_done = p.now("act")
                dve_proj_done = p.now("dve")

                if h == 0 and "v0" in tap_d:
                    p.barrier()
                    OP("sync", lambda e: e.dma_start(tap_d["v0"], vA[:, :]), inc=("qs", 16))
                    tap_flush()

                for qc in range(NQC):
                    kp = qc % 2
                    W("sync", "dve", keep_mark[kp])
                    W("sync", "sc", sc_thr[qc])
                    OP("sync", lambda e, qc=qc, kp=kp: e.dma_start(
                        keep8[:, kp * KT * QCW:(kp + 1) * KT * QCW].rearrange(
                            "p (kt w) -> p kt w", w=QCW),
                        mask_d[0:S, :].rearrange("(kt p) (nq w) -> p kt nq w", p=128, w=QCW)[:, :, qc, :]),
                       inc=("qs", 16))
                    qs_keep = p.now("qs")
                    psp_step(7)
                    dve_pm_k = {}
                    zb = 3 + (qc % 2)
                    ob = 7

                    def emit_av(kt2):
                        i3b = kt2 % 4
                        W("tensor", "dve", dve_pm_k[kt2])
                        if kt2 == 0:
                            W("tensor", "dve", zb_mark[qc % 2])
                        OP("tensor", lambda e, i3b=i3b, kt2=kt2, zb=zb: e.matmul(
                            ps[zb][:, 0:QCW], vA[:, kt2 * 128:(kt2 + 1) * 128],
                            pbuf[:, i3b * QCW:(i3b + 1) * QCW],
                            start=(kt2 == 0), stop=(kt2 == KT - 1)),
                           inc=("pe", 1))
                        pb_mark[i3b] = p.now("pe")

                    for kt in range(KT):
                        i3 = kt % 4
                        sb_b = kt % 3
                        W("tensor", "act", scb_mark[sb_b])
                        W("tensor", "act", act_proj_done)
                        W("tensor", "dve", dve_proj_done)
                        for pi, (lK, lQ) in enumerate(((kTH, qTH), (kTH, qTL), (kTL, qTH))):
                            OP("tensor", lambda e, sb_b=sb_b, kt=kt, qc=qc, lK=lK, lQ=lQ, pi=pi:
                               e.matmul(
                                   ps[sb_b][:, 0:QCW], lK[0:64, kt * 128:(kt + 1) * 128],
                                   lQ[0:64, qc * QCW:(qc + 1) * QCW],
                                   start=(pi == 0), stop=(pi == 2)),
                               inc=("pe", 1) if pi == 2 else None)
                        pe_sc = p.now("pe")
                        W("scalar", "pe", pe_sc)
                        W("scalar", "pe", pb_mark[i3])
                        OP("scalar", lambda e, i3=i3, sb_b=sb_b: e.activation(
                            pbuf[:, i3 * QCW:(i3 + 1) * QCW], ps[sb_b][:, 0:QCW], AF.Exp, scale=SCALE),
                           inc=("act", 1))
                        scb_mark[sb_b] = p.now("act")
                        W("vector", "act", p.now("act"))
                        W("vector", "qs", qs_keep)
                        OP("vector", lambda e, i3=i3, kt=kt, kp=kp: e.tensor_tensor(
                            pbuf[:, i3 * QCW:(i3 + 1) * QCW], pbuf[:, i3 * QCW:(i3 + 1) * QCW],
                            keep8[:, (kp * KT + kt) * QCW:(kp * KT + kt + 1) * QCW], ALU.mult))
                        dve_pm_k[kt] = p.now("dve")
                        if kt >= 3:
                            emit_av(kt - 3)
                    emit_av(KT - 3)
                    emit_av(KT - 2)
                    emit_av(KT - 1)
                    keep_mark[kp] = p.now("dve")
                    pe_av_done = p.now("pe")
                    W("vector", "pe", pe_av_done)
                    OP("vector", lambda e, zb=zb: e.reciprocal(srow[0:1, :], ps[zb][0:1, 0:QCW]))
                    dve_srow = p.now("dve")
                    W("tensor", "dve", dve_srow)
                    W("tensor", "act", ob_mark[0])
                    OP("tensor", lambda e, ob=ob: e.matmul(
                        ps[ob][:, 0:QCW], onesr[0:1, :], srow[0:1, :],
                        start=True, stop=True), inc=("pe", 1))
                    pe_bc = p.now("pe")
                    dcb = h // 2
                    W("scalar", "pe", pe_bc)
                    W("scalar", "dve", bcS_free[0])
                    OP("scalar", lambda e, ob=ob: e.copy(bcS[64:128, :], ps[ob][64:128, 0:QCW]),
                       inc=("act", 1))
                    act_bc = p.now("act")
                    ob_mark[0] = act_bc
                    W("vector", "act", act_bc)
                    if h % 2 == 1:
                        OP("vector", lambda e, zb=zb, dcb=dcb, qc=qc: e.tensor_tensor(
                            tm[64:128, dcb * S + qc * QCW:dcb * S + (qc + 1) * QCW],
                            ps[zb][64:128, 0:QCW], bcS[64:128, :], ALU.mult))
                    else:
                        W("vector", "qa", zst_free[0])
                        OP("vector", lambda e, zb=zb: e.tensor_tensor(
                            zstage[64:128, :], ps[zb][64:128, 0:QCW], bcS[64:128, :],
                            ALU.mult))
                        W("scalar", "dve", p.now("dve"))
                        OP("scalar", lambda e, dcb=dcb, qc=qc: e.dma_start(
                            tm[0:64, dcb * S + qc * QCW:dcb * S + (qc + 1) * QCW],
                            zstage[64:128, :]), inc=("qa", 16))
                        zst_free[0] = p.now("qa")
                    bcS_free[0] = p.now("dve")
                    zb_mark[qc % 2] = p.now("dve")
                    zb_mark[(qc + 1) % 2] = p.now("dve")
                qk_evac_guard = p.now("pe")
                wq_mark[hp] = p.now("pe")

            if "z_all" in tap_d:
                p.barrier()
                OP("sync", lambda e: e.dma_start(tap_d["z_all"], tm[:, :]), inc=("qs", 16))
                tap_flush()

        psp_flush()
        if go("B2"):
            p.barrier()
            woS2 = X[:, 2048:10240]
            for doc in range(2):
                OP("sync", lambda e, doc=doc: e.dma_start(
                    woS2[:, doc * 4096:(doc + 1) * 4096].rearrange("p (dc w) -> p dc w", w=512),
                    woT_d.rearrange("(dc p) (dd w) -> p dc dd w", p=128, w=512)[:, :, doc, :]),
                   inc=("qs", 16))
            qs_wo = p.now("qs")
            resb_mark = [0, 0]
            b2_act = [0] * TT
            b2_dve = [0] * TT

            def b2_res(t):
                pr = t % 2
                W("scalar", "dve", xt_mark[pr])
                OP("scalar", lambda e, t=t, pr=pr: e.dma_start(xt[pr], x_d[t * 128:(t + 1) * 128, :]),
                   inc=("qa", 16))
                qa_x = p.now("qa")
                W("vector", "qa", qa_x)
                OP("vector", lambda e, t=t, pr=pr: e.tensor_scalar(
                    lntile[:, :], xt[pr][:, :], m1[:, t:t + 1], r1[:, t:t + 1],
                    ALU.subtract, ALU.mult))
                OP("vector", lambda e, t=t, pr=pr: e.tensor_tensor(
                    fm[:, t * D:(t + 1) * D], xt[pr][:, :], lntile[:, :], ALU.add))
                xt_mark[pr] = p.now("dve")
                b2_dve[t] = p.now("dve")

            def b2_wo(t):
                for doc in range(2):
                    b = doc
                    W("tensor", "qs", qs_wo)
                    W("tensor", "dve", resb_mark[b])
                    for dc in range(DC):
                        OP("tensor", lambda e, b=b, dc=dc, t=t, doc=doc: e.matmul(
                            ps[b][:, :], tm[:, dc * S + t * 128:dc * S + (t + 1) * 128],
                            woS2[:, doc * 4096 + dc * 512:doc * 4096 + (dc + 1) * 512],
                            start=(dc == 0), stop=(dc == DC - 1)),
                           inc=("pe", 1) if dc == DC - 1 else None)
                    W("vector", "pe", p.now("pe"))
                    W("vector", "dve", b2_dve[t])
                    OP("vector", lambda e, b=b, t=t, doc=doc: e.tensor_tensor(
                        fm[:, t * D + doc * 512:t * D + (doc + 1) * 512],
                        fm[:, t * D + doc * 512:t * D + (doc + 1) * 512],
                        ps[b][:, :], ALU.add))
                    resb_mark[b] = p.now("dve")

            for t in range(TT):
                b2_res(t)
                if t >= 1:
                    b2_wo(t - 1)
            b2_wo(TT - 1)
            if "res_attn" in tap_d:
                p.barrier()
                OP("sync", lambda e: e.dma_start(tap_d["res_attn"], fm[:, :]), inc=("qs", 16))
                tap_flush()

        if go("CD"):
            p.barrier()
            OP("sync", lambda e: e.dma_start(gwS[:, :].rearrange("p (dc w) -> p dc w", w=E),
                                             gwT_d.rearrange("(dc p) ee -> p dc ee", p=128)),
               inc=("qs", 16))
            OP("sync", lambda e: e.dma_start(ebias_s[:, :], ebias_d), inc=("qs", 16))
            qs_gw = p.now("qs")
            W("vector", "qs", qs_gw)
            OP("vector", lambda e: e.tensor_copy(gwH[:, :], gwS[:, :]))
            OP("vector", lambda e: e.scalar_tensor_tensor(gwL[:, :], gwH[:, :], -1.0,
                                                          gwS[:, :], ALU.mult, ALU.add))
            dve_gw = p.now("dve")
            W("tensor", "qs", qs_gw)
            OP("tensor", lambda e: e.matmul(ps[7][:, 0:E], onesr[0:1, :], ebias_s[0:1, :],
                                            start=True, stop=True), inc=("pe", 1))
            W("vector", "pe", p.now("pe"))
            OP("vector", lambda e: e.tensor_copy(biasbc[:, :], ps[7][:, 0:E]))
            dve_biasbc = p.now("dve")

            gb_mark = [0, 0]
            lnt_mark2 = [0]
            cd_actmark = [0] * TT
            cd_dvemark = [0] * TT

            def emit_cd(t):
                OP("vector", lambda e, t=t: e.bn_stats(bnbuf[:, 0:6], fm[:, t * D:t * D + 512]))
                OP("vector", lambda e, t=t: e.bn_stats(bnbuf[:, 6:12], fm[:, t * D + 512:(t + 1) * D]))
                OP("vector", lambda e: e.bn_aggr(mv[:, :], bnbuf[:, :]))
                W("scalar", "dve", p.now("dve"))
                OP("scalar", lambda e: e.activation(rtile[:, 0:1], mv[:, 1:2], AF.Sqrt, bias=c_eps[:, 0:1]),
                   inc=("act", 1))
                W("vector", "act", p.now("act"))
                OP("vector", lambda e, t=t: e.reciprocal(r2[:, t:t + 1], rtile[:, 0:1]))
                OP("vector", lambda e, t=t: e.tensor_copy(m2[:, t:t + 1], mv[:, 0:1]))
                W("vector", "pe", lnt_mark2[0])
                OP("vector", lambda e, t=t: e.tensor_scalar(
                    lntile[:, :], fm[:, t * D:(t + 1) * D], m2[:, t:t + 1], r2[:, t:t + 1],
                    ALU.subtract, ALU.mult))
                dve_ln2 = p.now("dve")
                for dc in range(DC):
                    b = trb[dc % 2]
                    W("tensor", "act", trbA[dc % 2])
                    W("tensor", "dve", trbV[dc % 2])
                    W("tensor", "dve", dve_ln2)
                    OP("tensor", lambda e, b=b, dc=dc: e.transpose(
                        ps[b][:, 0:128], lntile[:, dc * 128:(dc + 1) * 128], ident[:, :]),
                       inc=("pe", 1))
                    pe_tr = p.now("pe")
                    W("scalar", "pe", pe_tr)
                    OP("scalar", lambda e, b=b, dc=dc, t=t: e.copy(
                        fmH[:, dc * S + t * 128:dc * S + (t + 1) * 128], ps[b][:, 0:128]),
                       inc=("act", 1))
                    trbA[dc % 2] = p.now("act")
                    W("vector", "pe", pe_tr)
                    W("vector", "act", trbA[dc % 2])
                    OP("vector", lambda e, b=b, dc=dc, t=t: e.scalar_tensor_tensor(
                        fmL[:, dc * S + t * 128:dc * S + (t + 1) * 128],
                        fmH[:, dc * S + t * 128:dc * S + (t + 1) * 128], -1.0,
                        ps[b][:, 0:128], ALU.mult, ALU.add))
                    trbV[dc % 2] = p.now("dve")
                lnt_mark2[0] = p.now("pe")
                act_ln2T_t = p.now("act")
                dve_ln2T_t = p.now("dve")
                b = 2 + (t % 2)
                W("tensor", "qs", qs_gw)
                W("tensor", "act", act_ln2T_t)
                W("tensor", "dve", dve_ln2T_t)
                W("tensor", "dve", dve_gw)
                W("tensor", "act", gb_mark[t % 2])
                np_ = 0
                for lA, lB in ((fmH, gwH), (fmH, gwL), (fmL, gwH)):
                    for dc in range(DC):
                        np_ += 1
                        OP("tensor", lambda e, b=b, dc=dc, t=t, lA=lA, lB=lB, np_=np_: e.matmul(
                            ps[b][:, 0:E], lA[:, dc * S + t * 128:dc * S + (t + 1) * 128],
                            lB[:, dc * E:(dc + 1) * E], start=(np_ == 1), stop=(np_ == 3 * DC)),
                           inc=("pe", 1) if np_ == 3 * DC else None)
                W("scalar", "pe", p.now("pe"))
                OP("scalar", lambda e, b=b: e.activation(sctile[:, :], ps[b][:, 0:E], AF.Sigmoid),
                   inc=("act", 1))
                gb_mark[t % 2] = p.now("act")
                W("vector", "act", p.now("act"))
                W("vector", "dve", dve_biasbc)
                OP("vector", lambda e: e.tensor_tensor(seltile[:, :], sctile[:, :], biasbc[:, :], ALU.add))
                OP("vector", lambda e: e.tensor_copy(work[:, :], seltile[:, :]))
                for it in range(TOPK):
                    OP("vector", lambda e: e.reduce_max(t8[:, :], work[:, :], axis=AX.X))
                    if it < TOPK - 1:
                        OP("vector", lambda e: e.tensor_scalar(eqt[:, :], work[:, :], t8[:, 0:1], None,
                                                               ALU.is_equal))
                        OP("vector", lambda e: e.scalar_tensor_tensor(work[:, :], eqt[:, :], -1e9,
                                                                      work[:, :], ALU.mult, ALU.add))
                OP("vector", lambda e: e.tensor_scalar(eqt[:, :], seltile[:, :], t8[:, 0:1], None,
                                                       ALU.is_ge))
                OP("vector", lambda e: e.tensor_tensor(work[:, :], sctile[:, :], eqt[:, :], ALU.mult))
                OP("vector", lambda e: e.reduce_sum(s8[:, :], work[:, :], axis=AX.X))
                OP("vector", lambda e: e.reciprocal(s8[:, :], s8[:, :]))
                OP("vector", lambda e, t=t: e.tensor_scalar(
                    comb[:, t * E:(t + 1) * E], work[:, :], s8[:, 0:1], ROUTE_SCALE,
                    ALU.mult, ALU.mult))
                cd_actmark[t] = p.now("act")
                cd_dvemark[t] = p.now("dve")

            if not go("E"):
                for _t in range(TT):
                    emit_cd(_t)
            if "ln2T" in tap_d:
                p.barrier()
                OP("sync", lambda e: e.dma_start(tap_d["ln2T"], tm[:, :]), inc=("qs", 16))
                tap_flush()
            if "combine" in tap_d:
                p.barrier()
                OP("sync", lambda e: e.dma_start(tap_d["combine"], comb[:, :]), inc=("qs", 16))
                tap_flush()

        if go("E"):
            W("sync", "pe")
            W("sync", "dve")
            _pe0 = p.now("pe")
            wg_mark = [_pe0, _pe0]
            wu_mark = [_pe0, _pe0]
            wd_mark = [_pe0, _pe0]
            hb_mark = [[0, 0], [0, 0]]
            htldF_mark = [0]
            yb_mark = [0] * 8

            for e_i in range(n_experts):
                shared = (e_i == E)
                for tc in range(NTC):
                    if e_i == 0:
                        W("tensor", "act")
                        W("tensor", "dve")
                        groups = [0, 1] if tc == 0 else ([tc + 1] if tc + 1 < NTC else [])
                        for g in groups:
                            for _t in range(g * 4, g * 4 + 4):
                                emit_cd(_t)
                    W("tensor", "act", cd_actmark[tc * 4 + 3])
                    W("tensor", "dve", cd_dvemark[tc * 4 + 3])
                    for ic in range(DC):
                        wp = ic % 2
                        wgi = (e_i * 3 + 0) * DC + ic
                        wui = (e_i * 3 + 1) * DC + ic
                        W("sync", "pe", wg_mark[wp])
                        W("sync", "qg", wsp_ready[wgi])
                        OP("sync", lambda e, wgi=wgi, wp=wp: e.dma_start(wgHL[wp][:, :], wsp_d[wgi]),
                           inc=("qs", 16))
                        W("sync", "pe", wu_mark[wp])
                        W("sync", "qg", wsp_ready[wui])
                        OP("sync", lambda e, wui=wui, wp=wp: e.dma_start(wuHL[wp][:, :], wsp_d[wui]),
                           inc=("qs", 16))
                        qs_w = p.now("qs")
                        gb0 = 2 * (ic % 2)
                        gb1 = gb0 + 1
                        W("tensor", "qs", qs_w)
                        W("tensor", "act", hb_mark[ic % 2][0])
                        W("tensor", "dve", hb_mark[ic % 2][1])
                        if ic < 2:
                            for bb in range(4):
                                W("tensor", "dve", yb_mark[bb])
                        np_ = 0
                        for lA, lB in ((wgH, fmH), (wgL, fmH), (wgH, fmL)):
                            for dc in range(DC):
                                np_ += 1
                                OP("tensor", lambda e, dc=dc, tc=tc, wp=wp, lA=lA, lB=lB, np_=np_, gb0=gb0: e.matmul(
                                    ps[gb0][:, :], lA[wp][:, dc * 128:(dc + 1) * 128],
                                    lB[:, dc * S + tc * TCW:dc * S + (tc + 1) * TCW],
                                    start=(np_ == 1), stop=(np_ == 3 * DC)),
                                   inc=("pe", 1) if np_ == 3 * DC else None)
                        np_ = 0
                        for lA, lB in ((wuH, fmH), (wuL, fmH), (wuH, fmL)):
                            for dc in range(DC):
                                np_ += 1
                                OP("tensor", lambda e, dc=dc, tc=tc, wp=wp, lA=lA, lB=lB, np_=np_, gb1=gb1: e.matmul(
                                    ps[gb1][:, :], lA[wp][:, dc * 128:(dc + 1) * 128],
                                    lB[:, dc * S + tc * TCW:dc * S + (tc + 1) * TCW],
                                    start=(np_ == 1), stop=(np_ == 3 * DC)),
                                   inc=("pe", 1) if np_ == 3 * DC else None)
                        wg_mark[wp] = p.now("pe")
                        wu_mark[wp] = p.now("pe")
                        pe_gu = p.now("pe")
                        W("scalar", "pe", pe_gu)
                        W("scalar", "dve", htldF_mark[0])
                        OP("scalar", lambda e, ic=ic, gb0=gb0: e.activation(
                            htldF[:, :], ps[gb0][:, :], AF.Silu), inc=("act", 1))
                        hb_mark[ic % 2][0] = p.now("act")
                        W("vector", "act", p.now("act"))
                        OP("vector", lambda e, ic=ic, gb1=gb1: e.tensor_tensor(
                            htldF[:, :], htldF[:, :], ps[gb1][:, :], ALU.mult))
                        hb_mark[ic % 2][1] = p.now("dve")
                        OP("vector", lambda e, ic=ic: e.tensor_copy(
                            htldH[:, ic * TCW:(ic + 1) * TCW], htldF[:, :]))
                        OP("vector", lambda e, ic=ic: e.scalar_tensor_tensor(
                            htldL[:, ic * TCW:(ic + 1) * TCW], htldH[:, ic * TCW:(ic + 1) * TCW],
                            -1.0, htldF[:, :], ALU.mult, ALU.add))
                        htldF_mark[0] = p.now("dve")
                    dve_h = p.now("dve")
                    if e_i == 0 and tc == 0 and "htld0" in tap_d:
                        p.barrier()
                        OP("sync", lambda e: e.dma_start(tap_d["htld0"],
                                                         htldH[:, :].bitcast(F32)), inc=("qs", 16))
                        tap_flush()
                    for ic in range(DC):
                        wp = ic % 2
                        wdi = (e_i * 3 + 2) * DC + ic
                        W("sync", "pe", wd_mark[wp])
                        W("sync", "qg", wsp_ready[wdi])
                        OP("sync", lambda e, wdi=wdi, wp=wp: e.dma_start(wdHL[wp][:, :], wsp_d[wdi]),
                           inc=("qs", 16))
                        W("tensor", "qs", p.now("qs"))
                        W("tensor", "dve", dve_h)
                        if ic == 0:
                            for bb in range(8):
                                W("tensor", "dve", yb_mark[bb])
                        for pi, (lA, lB) in enumerate(((htldH, wdH), (htldL, wdH), (htldH, wdL))):
                            for ts4 in range(4):
                                for doc in range(2):
                                    bb = ts4 * 2 + doc
                                    OP("tensor", lambda e, bb=bb, ic=ic, ts4=ts4, doc=doc, wp=wp,
                                       lA=lA, lB=lB, pi=pi: e.matmul(
                                        ps[bb][:, :], lA[:, ic * TCW + ts4 * 128:ic * TCW + (ts4 + 1) * 128],
                                        lB[wp][:, doc * 512:(doc + 1) * 512],
                                        start=(ic == 0 and pi == 0), stop=(ic == DC - 1 and pi == 2)),
                                       inc=("pe", 1) if bb == 7 and pi == 2 else None)
                        wd_mark[wp] = p.now("pe")
                    pe_down = p.now("pe")
                    for ts4 in range(4):
                        for doc in range(2):
                            bb = ts4 * 2 + doc
                            tt_i = tc * 4 + ts4
                            W("vector", "pe", pe_down)
                            if shared:
                                OP("vector", lambda e, bb=bb, tt_i=tt_i, doc=doc: e.tensor_tensor(
                                    fm[:, tt_i * D + doc * 512:tt_i * D + (doc + 1) * 512],
                                    fm[:, tt_i * D + doc * 512:tt_i * D + (doc + 1) * 512],
                                    ps[bb][:, :], ALU.add))
                            else:
                                OP("vector", lambda e, bb=bb, tt_i=tt_i, doc=doc, e_i=e_i:
                                   e.scalar_tensor_tensor(
                                       fm[:, tt_i * D + doc * 512:tt_i * D + (doc + 1) * 512],
                                       ps[bb][:, :], comb[:, tt_i * E + e_i:tt_i * E + e_i + 1],
                                       fm[:, tt_i * D + doc * 512:tt_i * D + (doc + 1) * 512],
                                       ALU.mult, ALU.add))
                            yb_mark[bb] = p.now("dve")
            if "res_ffn" in tap_d:
                p.barrier()
                OP("sync", lambda e: e.dma_start(tap_d["res_ffn"], fm[:, :]), inc=("qs", 16))
                tap_flush()

        if stop is None:
            p.barrier()
            for t in range(TT):
                OP("sync", lambda e, t=t: e.dma_start(out_d[t * 128:(t + 1) * 128, :],
                                                      fm[:, t * D:(t + 1) * D]), inc=("qs", 16))
            p.wait("sync", "qs", p.now("qs"))

        def make_body(ename):
            def body(eng):
                for fn, inc in p.q[ename]:
                    if isinstance(fn, tuple) and fn[0] == "wait":
                        eng.wait_ge(sems[fn[1]], fn[2])
                    else:
                        r = fn(eng)
                        if inc is not None:
                            r.then_inc(sems[inc[0]], inc[1])
            return body

        for ename in ENGS:
            getattr(block, ename)(make_body(ename))

    return nc


def _stage(inputs):
    f32 = np.float32
    minds = np.asarray(inputs["L0_attn_mask_inds"], np.int64)
    q = minds[0]
    k = minds[1]
    reg = (q // QCW).astype(np.int64)
    order = np.argsort(reg, kind="stable")
    qs_ = q[order]
    ks_ = k[order]
    regs = reg[order]
    q2 = np.zeros((MREG, MCAP), np.int32)
    k2 = np.full((MREG, MCAP), S, np.int32)
    for r in range(MREG):
        sel = regs == r
        n = int(sel.sum())
        assert n <= MCAP, f"mask region {r} overflow: {n}"
        q2[r, :n] = qs_[sel]
        k2[r, :n] = ks_[sel]
    q3 = q2.reshape(MREG, 128, RB).transpose(1, 0, 2).reshape(-1)
    k3 = k2.reshape(MREG, 128, RB).transpose(1, 0, 2).reshape(-1)
    minds2 = np.ascontiguousarray(np.stack([q3, k3]).astype(np.int32))

    base = {
        "minds": minds2,
        "wqT": np.ascontiguousarray(np.asarray(inputs["L0_attn_Wq"], f32).T),
        "wkT": np.ascontiguousarray(np.asarray(inputs["L0_attn_Wk"], f32).T),
        "wvT": np.ascontiguousarray(np.asarray(inputs["L0_attn_Wv"], f32).T),
        "woT": np.ascontiguousarray(np.asarray(inputs["L0_attn_Wo"], f32).T),
        "gwT": np.ascontiguousarray(np.asarray(inputs["L0_ffn_gate_w"], f32).T),
        "ebias": np.ascontiguousarray(np.asarray(inputs["L0_ffn_expert_bias"], f32).reshape(1, E)),
        "wgT": np.ascontiguousarray(np.asarray(inputs["L0_ffn_w_gate"], f32).transpose(0, 2, 1)),
        "wuT": np.ascontiguousarray(np.asarray(inputs["L0_ffn_w_up"], f32).transpose(0, 2, 1)),
        "wdT": np.ascontiguousarray(np.asarray(inputs["L0_ffn_w_down"], f32).transpose(0, 2, 1)),
        "sgT": np.ascontiguousarray(np.asarray(inputs["L0_ffn_shared_gate_w"], f32).T),
        "suT": np.ascontiguousarray(np.asarray(inputs["L0_ffn_shared_up_w"], f32).T),
        "sdT": np.ascontiguousarray(np.asarray(inputs["L0_ffn_shared_down_w"], f32).T),
    }
    x = np.asarray(inputs["L0_attn_x"], f32)
    return [dict(base, x=np.ascontiguousarray(x[b])) for b in range(8)]


def kernel(**inputs):
    in_maps = _stage(inputs)
    nc = build()
    r = run_bass_kernel_spmd(nc, in_maps, core_ids=list(range(8)))
    return np.stack([np.asarray(m["out"], np.float32) for m in r.results], axis=0)
